# version 02
# baseline (speedup 1.0000x reference)
import contextlib
import math
import numpy as np
import ml_dtypes
import concourse.bass as bass
import concourse.mybir as mybir
from concourse.bass_utils import run_bass_kernel_spmd

F32 = mybir.dt.float32
BF16 = mybir.dt.bfloat16
I32 = mybir.dt.int32
AF = mybir.ActivationFunctionType
ALU = mybir.AluOpType
AX = mybir.AxisListType
BF = ml_dtypes.bfloat16

H = 4096
T = 4096
NH = 32
NKV = 8
HD = 128
II = 11008
ICN = 86
HCN = 32
R = 32
TB = 512
SLOT = [3, 4, 7, 8]
OWN = [[0, 1, 6, 7], [2, 3, 4, 5]]
SM_SCALE = 1.0 / math.sqrt(HD)
Z = ("pe", 0)


class Emitter:
    def __init__(self, nc, es):
        self.nc = nc
        self.sem = {}
        self.cnt = {}
        for s in ["sdma", "gdma", "pe", "act", "dve"]:
            self.sem[s] = es.enter_context(nc.semaphore(s))
            self.cnt[s] = 0
        self.block = None
        self.last_dmat = 0

    def emit(self, eng, f, waits=(), inc=None):
        amt = 16 if inc in ("sdma", "gdma") else 1
        if inc is not None:
            self.cnt[inc] += amt
        tgt = self.cnt[inc] if inc is not None else None
        sems = self.sem
        wl = tuple((s, v) for (s, v) in waits if v > 0)

        def body(e, f=f, wl=wl, inc=inc, amt=amt):
            for s, v in wl:
                e.wait_ge(sems[s], v)
            r = f(e)
            if inc is not None:
                r.then_inc(sems[inc], amt)
        getattr(self.block, eng)(body)
        return tgt

    def dma(self, q, out, in_, waits=()):
        eng = "sync" if q == "s" else "gpsimd"
        sem = "sdma" if q == "s" else "gdma"
        w = list(waits)
        if q == "s" and self.last_dmat > 0:
            w.append(("sdma", self.last_dmat))
        return self.emit(eng, lambda e, o=out, i=in_: e.dma_start(out=o, in_=i),
                         waits=w, inc=sem)

    def dmat(self, out, in_, waits=()):
        w = list(waits) + [("sdma", self.cnt["sdma"])]
        tgt = self.emit("sync", lambda e, o=out, i=in_: e.dma_start_transpose(out=o, in_=i),
                        waits=w, inc="sdma")
        self.last_dmat = tgt
        return tgt

    def mmg(self, psum, pairs, waits=()):
        def f(e, psum=psum, pairs=list(pairs)):
            n = len(pairs)
            r = None
            for i, (lhsT, rhs) in enumerate(pairs):
                r = e.matmul(psum, lhsT=lhsT, rhs=rhs, start=(i == 0), stop=(i == n - 1),
                             skip_group_check=True)
            return r
        return self.emit("tensor", f, waits=waits, inc="pe")



def _sbl(nc, es, name, n, shape, dt):
    return [es.enter_context(nc.sbuf_tensor(f"{name}_{i}", shape, dt)) for i in range(n)]


def _psl(nc, es, name, n, shape, dt):
    return [es.enter_context(nc.psum_tensor(f"{name}_{i}", shape, dt)) for i in range(n)]


def build_program():
    nc = bass.Bass("TRN2", target_bir_lowering=False)
    ap = lambda *a, **k: nc.dram_tensor(*a, **k).ap()

    x_full = ap("x_full", [T, H], F32, kind="ExternalInput")
    xq = ap("xq", [2048, H], F32, kind="ExternalInput")
    wq_w = ap("wq_w", [H, H], F32, kind="ExternalInput")
    wk_w = ap("wk_w", [1024, H], F32, kind="ExternalInput")
    wv_w = ap("wv_w", [1024, H], F32, kind="ExternalInput")
    wo_w = ap("wo_w", [H, H], F32, kind="ExternalInput")
    gq_w = ap("gq_w", [II, H], I32, kind="ExternalInput")
    uq_w = ap("uq_w", [II, H], I32, kind="ExternalInput")
    dq_w = ap("dq_w", [H, II], I32, kind="ExternalInput")
    gs_pc = ap("gs_pc", [128, ICN], F32, kind="ExternalInput")
    us_pc = ap("us_pc", [128, ICN], F32, kind="ExternalInput")
    ds_pc = ap("ds_pc", [128, HCN], F32, kind="ExternalInput")
    gA = ap("gA", [128, HCN, R], F32, kind="ExternalInput")
    uA = ap("uA", [128, HCN, R], F32, kind="ExternalInput")
    dA = ap("dA", [128, ICN, R], F32, kind="ExternalInput")
    gB = ap("gB", [R, II], F32, kind="ExternalInput")
    uB = ap("uB", [R, II], F32, kind="ExternalInput")
    dB = ap("dB", [R, H], F32, kind="ExternalInput")
    cosk = ap("cosk", [64, T], F32, kind="ExternalInput")
    sink = ap("sink", [64, T], F32, kind="ExternalInput")
    cosq = ap("cosq", [64, 2048], F32, kind="ExternalInput")
    sinq = ap("sinq", [64, 2048], F32, kind="ExternalInput")
    iota = ap("iota", [128, T], F32, kind="ExternalInput")
    theta = ap("theta", [128, 16], F32, kind="ExternalInput")
    identf = ap("identf", [128, 128], F32, kind="ExternalInput")
    identb = ap("identb", [128, 128], BF16, kind="ExternalInput")
    onescol = ap("onescol", [128, 1], BF16, kind="ExternalInput")
    onesrow = ap("onesrow", [1, 128], BF16, kind="ExternalInput")

    y = ap("y", [2048, H], F32, kind="ExternalOutput")

    s_wkT = ap("s_wkT", [8, 128, HCN, 128], BF16, kind="Internal")
    s_wvT = ap("s_wvT", [8, 128, HCN, 128], BF16, kind="Internal")
    s_wqT = ap("s_wqT", [NH, 128, HCN, 128], BF16, kind="Internal")
    s_woT = ap("s_woT", [HCN, 128, NH, 128], BF16, kind="Internal")
    s_gqT = ap("s_gqT", [ICN, 128, HCN + 1, 128], BF16, kind="Internal")
    s_uqT = ap("s_uqT", [ICN, 128, HCN + 1, 128], BF16, kind="Internal")
    s_dqT = ap("s_dqT", [HCN, 128, ICN + 1, 128], BF16, kind="Internal")
    s_k = ap("s_k", [NKV, 128, T], BF16, kind="Internal")
    s_v = ap("s_v", [NKV, 128, 32, 128], BF16, kind="Internal")
    s_xnfq = ap("s_xnfq", [4, 128, HCN, TB], BF16, kind="Internal")
    s_o = ap("s_o", [4, NH, 128, TB], BF16, kind="Internal")

    prog_es = contextlib.ExitStack()
    E = Emitter(nc, prog_es)
    sb_identf = prog_es.enter_context(nc.sbuf_tensor("c_identf", [128, 128], F32))
    sb_identb = prog_es.enter_context(nc.sbuf_tensor("c_identb", [128, 128], BF16))
    sb_ones = prog_es.enter_context(nc.sbuf_tensor("c_ones", [128, 1], BF16))
    sb_onesr = prog_es.enter_context(nc.sbuf_tensor("c_onesr", [1, 128], BF16))
    sb_theta = prog_es.enter_context(nc.sbuf_tensor("c_theta", [128, 16], F32))
    sb_gs = prog_es.enter_context(nc.sbuf_tensor("c_gs", [128, ICN], F32))
    sb_us = prog_es.enter_context(nc.sbuf_tensor("c_us", [128, ICN], F32))
    sb_ds = prog_es.enter_context(nc.sbuf_tensor("c_ds", [128, HCN], F32))
    sb_cgc = prog_es.enter_context(nc.sbuf_tensor("c_cgc", [128, ICN], F32))
    sb_cuc = prog_es.enter_context(nc.sbuf_tensor("c_cuc", [128, ICN], F32))
    sb_cdc = prog_es.enter_context(nc.sbuf_tensor("c_cdc", [128, HCN], F32))

    stage_done = {}

    with contextlib.ExitStack() as es:
        blk = es.enter_context(nc.Block("b_stage0"))
        E.block = blk
        E.last_dmat = 0

        for dst, src in [(sb_identf, identf), (sb_identb, identb), (sb_ones, onescol),
                         (sb_onesr, onesrow), (sb_theta, theta)]:
            E.dma("g", dst[:], src)
        g_gs = E.dma("g", sb_gs[:], gs_pc)
        g_us = E.dma("g", sb_us[:], us_pc)
        g_ds = E.dma("g", sb_ds[:], ds_pc)
        a1 = E.emit("scalar", lambda e: e.mul(out=sb_gs[:], in_=sb_gs[:], mul=0.005),
                    waits=[("gdma", g_gs)], inc="act")
        a2 = E.emit("scalar", lambda e: e.mul(out=sb_us[:], in_=sb_us[:], mul=0.005),
                    waits=[("gdma", g_us)], inc="act")
        a3 = E.emit("scalar", lambda e: e.mul(out=sb_ds[:], in_=sb_ds[:], mul=0.005),
                    waits=[("gdma", g_ds)], inc="act")
        a4 = E.emit("scalar", lambda e: e.mul(out=sb_cgc[:], in_=sb_gs[:], mul=-7.5),
                    waits=[("act", a1)], inc="act")
        a5 = E.emit("scalar", lambda e: e.mul(out=sb_cuc[:], in_=sb_us[:], mul=-7.5),
                    waits=[("act", a2)], inc="act")
        scales_ready = E.emit("scalar", lambda e: e.mul(out=sb_cdc[:], in_=sb_ds[:], mul=-7.5),
                              waits=[("act", a3)], inc="act")

        st_nat_f = _sbl(nc, es, "st_nat_f", 1, [128, 4096], F32)
        st_nat_i = _sbl(nc, es, "st_nat_i", 1, [128, 4096], I32)
        st_bf = _sbl(nc, es, "st_bf", 2, [128, 4096], BF16)
        st_to = _sbl(nc, es, "st_to", 2, [128, 32, 128], BF16)
        st_bn = _sbl(nc, es, "st_bn", 2, [32, 128], F32)
        st_bb = _sbl(nc, es, "st_bb", 2, [32, 128], BF16)
        stg = {"nat": [Z, Z], "bf": [Z, Z], "bn": [Z, Z], "bb": [Z, Z]}

        def stage_weight(name, src, nrows, width, is_int, scale_sb, dst, bsrc=None):
            ncols = width
            nchunk = (ncols + 4095) // 4096
            last = 0
            for r in range(nrows):
                for cch in range(nchunk):
                    c0 = cch * 4096
                    cw = min(4096, ncols - c0)
                    mo = c0 // 128
                    mw = cw // 128
                    bi = (r * nchunk + cch) % 2
                    nat = (st_nat_i if is_int else st_nat_f)[0]
                    ld = E.dma("s", nat[:, 0:cw], src[r * 128:(r + 1) * 128, c0:c0 + cw],
                               waits=[stg["nat"][0]])
                    if is_int:
                        cv = E.emit(
                            "vector",
                            lambda e, bi=bi, cw=cw, nat=nat, r=r, scale_sb=scale_sb:
                            e.tensor_scalar(
                                out=st_bf[bi][:, 0:cw], in0=nat[:, 0:cw],
                                scalar1=scale_sb[:, r:r + 1], scalar2=None, op0=ALU.mult),
                            waits=[("sdma", ld), stg["bf"][bi], ("act", scales_ready)],
                            inc="dve")
                        cast = ("dve", cv)
                    else:
                        cv = E.emit(
                            "scalar",
                            lambda e, bi=bi, cw=cw, nat=nat: e.copy(
                                out=st_bf[bi][:, 0:cw], in_=nat[:, 0:cw]),
                            waits=[("sdma", ld), stg["bf"][bi]],
                            inc="act")
                        cast = ("act", cv)
                    stg["nat"][0] = cast
                    td = E.dmat(st_to[bi][:, 0:mw, :], st_bf[bi][:, 0:cw], waits=[cast])
                    stg["bf"][bi] = ("sdma", td)
                    last = E.dma("s", dst[r, :, mo:mo + mw, :], st_to[bi][:, 0:mw, :],
                                 waits=[("sdma", td)])
                if bsrc is not None:
                    bi = r % 2
                    lb = E.dma("s", st_bn[bi][:], bsrc[:, r * 128:(r + 1) * 128],
                               waits=[stg["bn"][bi]])
                    cb = E.emit("scalar", lambda e, bi=bi: e.copy(
                        out=st_bb[bi][:], in_=st_bn[bi][:]),
                        waits=[("sdma", lb), stg["bb"][bi]], inc="act")
                    stg["bn"][bi] = ("act", cb)
                    last = E.dma("s", dst[r, 0:32, ncols // 128, :], st_bb[bi][:],
                                 waits=[("act", cb)])
                    stg["bb"][bi] = ("sdma", last)
            stage_done[name] = last

        stage_weight("wk", wk_w, 8, 4096, False, None, s_wkT)
        stage_weight("wv", wv_w, 8, 4096, False, None, s_wvT)

        p0_x = _sbl(nc, es, "p0_x", 2, [128, H], F32)
        p0_xn = _sbl(nc, es, "p0_xn", 2, [128, H], BF16)
        p0_st = _sbl(nc, es, "p0_st", 2, [128, 48], F32)
        p0_ag = _sbl(nc, es, "p0_ag", 2, [128, 2], F32)
        p0_rs = _sbl(nc, es, "p0_rs", 2, [128, 2], F32)
        p0_xnf = es.enter_context(nc.sbuf_tensor("p0_xnf", [128, HCN, TB], BF16))
        p0_wst = _sbl(nc, es, "p0_wst", 2, [128, HCN, 128], BF16)
        p0_cos = _sbl(nc, es, "p0_cos", 2, [64, TB], F32)
        p0_sin = _sbl(nc, es, "p0_sin", 2, [64, TB], F32)
        p0_kro = _sbl(nc, es, "p0_kro", 4, [64, TB], F32)
        p0_kbf = _sbl(nc, es, "p0_kbf", 2, [128, TB], BF16)
        p0_vbf = _sbl(nc, es, "p0_vbf", 2, [128, TB], BF16)
        p0_vt = _sbl(nc, es, "p0_vt", 2, [128, 4, 128], BF16)
        ps_k = _psl(nc, es, "ps_k", 2, [128, TB], F32)
        ps_v = _psl(nc, es, "ps_v", 2, [128, TB], F32)

        ln_x_used = [Z, Z]
        ln_xn_used = [Z, Z]

        def emit_ln_tile(src_rows, bufi):
            xld = E.dma("g", p0_x[bufi][:], src_rows, waits=[ln_x_used[bufi]])
            v = E.emit("vector", lambda e, b=bufi: [
                e.bn_stats(out=p0_st[b][:, 6 * c:6 * (c + 1)],
                           in_=p0_x[b][:, 512 * c:512 * (c + 1)]) for c in range(8)][-1],
                waits=[("gdma", xld)], inc="dve")
            v = E.emit("vector", lambda e, b=bufi: e.bn_aggr(out=p0_ag[b][:], in_=p0_st[b][:]),
                       waits=[("dve", v)], inc="dve")
            v = E.emit("vector", lambda e, b=bufi: e.tensor_scalar(
                out=p0_rs[b][:, 0:1], in0=p0_ag[b][:, 1:2],
                scalar1=1e-5, scalar2=None, op0=ALU.add),
                waits=[("dve", v)], inc="dve")
            a = E.emit("scalar", lambda e, b=bufi: e.activation(
                out=p0_rs[b][:, 0:1], in_=p0_rs[b][:, 0:1], func=AF.Sqrt),
                waits=[("dve", v)], inc="act")
            v = E.emit("vector", lambda e, b=bufi: e.reciprocal(
                out=p0_rs[b][:, 1:2], in_=p0_rs[b][:, 0:1]),
                waits=[("act", a)], inc="dve")
            v = E.emit("vector", lambda e, b=bufi: e.tensor_scalar(
                out=p0_rs[b][:, 0:1], in0=p0_ag[b][:, 0:1],
                scalar1=p0_rs[b][:, 1:2], scalar2=-1.0, op0=ALU.mult, op1=ALU.mult),
                waits=[("dve", v)], inc="dve")
            v = E.emit("vector", lambda e, b=bufi: e.tensor_scalar(
                out=p0_xn[b][:], in0=p0_x[b][:],
                scalar1=p0_rs[b][:, 1:2], scalar2=p0_rs[b][:, 0:1],
                op0=ALU.mult, op1=ALU.add),
                waits=[("dve", v), ln_xn_used[bufi]], inc="dve")
            ln_x_used[bufi] = ("dve", v)
            return v

        xnf_consumed = Z
        kro_used = [Z, Z]
        kbf_used = [Z, Z]
        kstore = [Z, Z]
        wst_cons = [Z, Z]
        vbf_guard = [Z, Z]
        vt_store = [Z, Z]
        for t5 in range(8):
            last_td = 0
            for tt in range(4):
                bufi = tt % 2
                rows = x_full[t5 * TB + tt * 128: t5 * TB + (tt + 1) * 128, :]
                v = emit_ln_tile(rows, bufi)
                td = E.dmat(p0_xnf[:, :, tt * 128:(tt + 1) * 128], p0_xn[bufi][:],
                            waits=[("dve", v), xnf_consumed])
                ln_xn_used[bufi] = ("sdma", td)
                last_td = td
            cpar = t5 % 2
            cld = E.dma("g", p0_cos[cpar][:], cosk[:, t5 * TB:(t5 + 1) * TB],
                        waits=[kro_used[cpar]])
            sld = E.dma("g", p0_sin[cpar][:], sink[:, t5 * TB:(t5 + 1) * TB],
                        waits=[kro_used[cpar]])
            for r in range(NKV):
                bi = r % 2
                wst = E.dma("s", p0_wst[bi][:], s_wkT[r],
                            waits=[("sdma", stage_done["wk"]), wst_cons[bi]])
                mmk = E.mmg(ps_k[bi][:],
                            [(p0_wst[bi][:, hc, :], p0_xnf[:, hc, :]) for hc in range(HCN)],
                            waits=[("sdma", wst), ("sdma", last_td), kbf_used[bi]])
                t1 = E.emit("vector", lambda e, bi=bi, c=cpar: e.tensor_tensor(
                    out=p0_kro[0][:], in0=ps_k[bi][0:64, :], in1=p0_cos[c][:], op=ALU.mult),
                    waits=[("pe", mmk), ("gdma", sld), kstore[bi]], inc="dve")
                t2 = E.emit("vector", lambda e, bi=bi, c=cpar: e.tensor_tensor(
                    out=p0_kro[1][:], in0=ps_k[bi][64:128, :], in1=p0_sin[c][:], op=ALU.mult),
                    waits=[("dve", t1)], inc="dve")
                t3 = E.emit("vector", lambda e, bi=bi, c=cpar: e.tensor_tensor(
                    out=p0_kro[2][:], in0=ps_k[bi][64:128, :], in1=p0_cos[c][:], op=ALU.mult),
                    waits=[("dve", t2)], inc="dve")
                t4 = E.emit("vector", lambda e, bi=bi, c=cpar: e.tensor_tensor(
                    out=p0_kro[3][:], in0=ps_k[bi][0:64, :], in1=p0_sin[c][:], op=ALU.mult),
                    waits=[("dve", t3)], inc="dve")
                v5 = E.emit("vector", lambda e, bi=bi: e.tensor_sub(
                    out=p0_kbf[bi][0:64, :], in0=p0_kro[0][:], in1=p0_kro[1][:]),
                    waits=[("dve", t4)], inc="dve")
                v6 = E.emit("vector", lambda e, bi=bi: e.tensor_add(
                    out=p0_kbf[bi][64:128, :], in0=p0_kro[2][:], in1=p0_kro[3][:]),
                    waits=[("dve", v5)], inc="dve")
                kro_used[cpar] = ("dve", v6)
                kbf_used[bi] = ("dve", v6)
                kst = E.dma("g", s_k[r][:, t5 * TB:(t5 + 1) * TB], p0_kbf[bi][:],
                            waits=[("dve", v6)])
                kstore[bi] = ("gdma", kst)
                wsv = E.dma("s", p0_wst[bi][:], s_wvT[r],
                            waits=[("sdma", stage_done["wv"]), ("pe", mmk)])
                mmv = E.mmg(ps_v[bi][:],
                            [(p0_wst[bi][:, hc, :], p0_xnf[:, hc, :]) for hc in range(HCN)],
                            waits=[("sdma", wsv), vbf_guard[bi]])
                wst_cons[bi] = ("pe", mmv)
                av = E.emit("scalar", lambda e, bi=bi: e.copy(
                    out=p0_vbf[bi][:], in_=ps_v[bi][:]),
                    waits=[("pe", mmv), vbf_guard[bi]], inc="act")
                vtd = E.dmat(p0_vt[bi][:], p0_vbf[bi][:],
                             waits=[("act", av), vt_store[bi]])
                vbf_guard[bi] = ("sdma", vtd)
                vst = E.dma("g", s_v[r][:, t5 * 4:(t5 + 1) * 4, :], p0_vt[bi][:],
                            waits=[("sdma", vtd)])
                vt_store[bi] = ("gdma", vst)
                if r == NKV - 1:
                    xnf_consumed = ("pe", mmv)

        xnfq_store = Z
        for g in range(4):
            last_td = 0
            for tt in range(4):
                bufi = tt % 2
                rows = xq[g * TB + tt * 128: g * TB + (tt + 1) * 128, :]
                v = emit_ln_tile(rows, bufi)
                td = E.dmat(p0_xnf[:, :, tt * 128:(tt + 1) * 128], p0_xn[bufi][:],
                            waits=[("dve", v), xnf_consumed, xnfq_store])
                ln_xn_used[bufi] = ("sdma", td)
                last_td = td
            xnf_consumed = Z
            st = E.dma("g", s_xnfq[g], p0_xnf[:], waits=[("sdma", last_td)])
            xnfq_store = ("gdma", st)

        stage_weight("wq", wq_w, NH, 4096, False, None, s_wqT)
        stage_weight("wo", wo_w, HCN, 4096, False, None, s_woT)
        stage_weight("dq", dq_w, HCN, II, True, sb_ds, s_dqT, bsrc=dB)
        stage_weight("gq", gq_w, ICN, 4096, True, sb_gs, s_gqT, bsrc=gB)
        stage_weight("uq", uq_w, ICN, 4096, True, sb_us, s_uqT, bsrc=uB)

    for g in range(4):
        S = SLOT[g]
        L = S * TB
        NT = 4 * S
        with contextlib.ExitStack() as es:
            blk = es.enter_context(nc.Block(f"b_attn{g}"))
            E.block = blk
            E.last_dmat = 0

            a_xnf = es.enter_context(nc.sbuf_tensor(f"a_xnf{g}", [128, HCN, TB], BF16))
            a_q = es.enter_context(nc.sbuf_tensor(f"a_q{g}", [128, NH, TB], BF16))
            a_wst = _sbl(nc, es, f"a_wst{g}", 2, [128, HCN, 128], BF16)
            a_cosq = es.enter_context(nc.sbuf_tensor(f"a_cosq{g}", [64, TB], F32))
            a_sinq = es.enter_context(nc.sbuf_tensor(f"a_sinq{g}", [64, TB], F32))
            a_rot = _sbl(nc, es, f"a_rot{g}", 4, [64, TB], F32)
            a_k = _sbl(nc, es, f"a_k{g}", 2, [128, L], BF16)
            a_v = _sbl(nc, es, f"a_v{g}", 2, [128, NT, 128], BF16)
            a_iota = es.enter_context(nc.sbuf_tensor(f"a_iota{g}", [128, L], F32))
            a_p = _sbl(nc, es, f"a_p{g}", 4, [128, L], BF16)
            a_mx = es.enter_context(nc.sbuf_tensor(f"a_mx{g}", [128, 8], F32))
            a_m = es.enter_context(nc.sbuf_tensor(f"a_m{g}", [128, 4], F32))
            a_bias = es.enter_context(nc.sbuf_tensor(f"a_bias{g}", [128, 4], F32))
            a_l = es.enter_context(nc.sbuf_tensor(f"a_l{g}", [128, 4], F32))
            a_rl = es.enter_context(nc.sbuf_tensor(f"a_rl{g}", [128, 4], F32))
            a_pt = _sbl(nc, es, f"a_pt{g}", 2, [128, TB], BF16)
            a_o = _sbl(nc, es, f"a_o{g}", 2, [128, TB], BF16)
            ps_q = _psl(nc, es, f"ps_q{g}", 2, [128, TB], F32)
            ps_s = _psl(nc, es, f"ps_s{g}", 2, [128, TB], F32)
            ps_pt = _psl(nc, es, f"ps_pt{g}", 2, [128, TB], BF16)
            ps_o = es.enter_context(nc.psum_tensor(f"ps_o{g}", [128, TB], F32))

            ld_xnf = E.dma("g", a_xnf[:], s_xnfq[g])
            E.dma("g", a_cosq[:], cosq[:, g * TB:(g + 1) * TB])
            ld_sq = E.dma("g", a_sinq[:], sinq[:, g * TB:(g + 1) * TB])
            ld_iota = E.dma("g", a_iota[:], iota[:, 0:L])

            rot_used = 0
            qstrip_cons = [Z, Z]
            for h in range(NH):
                bi = h % 2
                wst = E.dma("s", a_wst[bi][:], s_wqT[h], waits=[qstrip_cons[bi]])
                mmq = E.mmg(ps_q[bi][:],
                            [(a_wst[bi][:, hc, :], a_xnf[:, hc, :]) for hc in range(HCN)],
                            waits=[("sdma", wst), ("gdma", ld_xnf), ("dve", rot_used)])
                qstrip_cons[bi] = ("pe", mmq)
                t1 = E.emit("vector", lambda e, bi=bi: e.tensor_tensor(
                    out=a_rot[0][:], in0=ps_q[bi][0:64, :], in1=a_cosq[:], op=ALU.mult),
                    waits=[("pe", mmq), ("gdma", ld_sq)], inc="dve")
                t2 = E.emit("vector", lambda e, bi=bi: e.tensor_tensor(
                    out=a_rot[1][:], in0=ps_q[bi][64:128, :], in1=a_sinq[:], op=ALU.mult),
                    waits=[("dve", t1)], inc="dve")
                t3 = E.emit("vector", lambda e, bi=bi: e.tensor_tensor(
                    out=a_rot[2][:], in0=ps_q[bi][64:128, :], in1=a_cosq[:], op=ALU.mult),
                    waits=[("dve", t2)], inc="dve")
                t4 = E.emit("vector", lambda e, bi=bi: e.tensor_tensor(
                    out=a_rot[3][:], in0=ps_q[bi][0:64, :], in1=a_sinq[:], op=ALU.mult),
                    waits=[("dve", t3)], inc="dve")
                v5 = E.emit("vector", lambda e, h=h: e.tensor_sub(
                    out=a_q[0:64, h, :], in0=a_rot[0][:], in1=a_rot[1][:]),
                    waits=[("dve", t4)], inc="dve")
                v6 = E.emit("vector", lambda e, h=h: e.tensor_add(
                    out=a_q[64:128, h, :], in0=a_rot[2][:], in1=a_rot[3][:]),
                    waits=[("dve", v5)], inc="dve")
                rot_used = v6
            q_ready = rot_used

            k_used = [Z, Z]
            v_used = [Z, Z]
            ps_s_reader = [Z, Z]
            pt_evac = [Z, Z]
            pt_cons = [Z, Z]
            o_evac = Z
            o_store = [Z, Z]
            exp_prev = 0
            last_av = 0
            for kh in range(NKV):
                kb = kh % 2
                ldk = E.dma("s", a_k[kb][:], s_k[kh][:, 0:L], waits=[k_used[kb]])
                ldv = E.dma("s", a_v[kb][:, :, :], s_v[kh][:, 0:NT, :], waits=[v_used[kb]])
                for h in range(4 * kh, 4 * kh + 4):
                    for j in range(4):
                        for c in range(S):
                            sb = (j * S + c) % 2
                            mm = E.mmg(ps_s[sb][:],
                                       [(a_q[:, h, j * 128:(j + 1) * 128],
                                         a_k[kb][:, c * TB:(c + 1) * TB])],
                                       waits=[("sdma", ldk), ("dve", q_ready),
                                              ps_s_reader[sb]])
                            rd = E.emit("vector", lambda e, sb=sb, c=c: e.reduce_max(
                                out=a_mx[:, c:c + 1], in_=ps_s[sb][:], axis=AX.X),
                                waits=[("pe", mm)], inc="dve")
                            ps_s_reader[sb] = ("dve", rd)
                        rd = E.emit("vector", lambda e, j=j, S=S: e.reduce_max(
                            out=a_m[:, j:j + 1], in_=a_mx[:, 0:S], axis=AX.X),
                            waits=[("dve", E.cnt["dve"])], inc="dve")
                        E.emit("vector", lambda e, j=j: e.tensor_scalar(
                            out=a_bias[:, j:j + 1], in0=a_m[:, j:j + 1],
                            scalar1=-SM_SCALE, scalar2=None, op0=ALU.mult),
                            waits=[("dve", rd), ("act", exp_prev)], inc="dve")
                    bias_rdy = E.cnt["dve"]
                    p_ready = []
                    for j in range(4):
                        for c in range(S):
                            sb = (j * S + c) % 2
                            mm = E.mmg(ps_s[sb][:],
                                       [(a_q[:, h, j * 128:(j + 1) * 128],
                                         a_k[kb][:, c * TB:(c + 1) * TB])],
                                       waits=[ps_s_reader[sb]])
                            ex = E.emit("scalar", lambda e, sb=sb, j=j, c=c: e.activation(
                                out=a_p[j][:, c * TB:(c + 1) * TB], in_=ps_s[sb][:],
                                func=AF.Exp, bias=a_bias[:, j:j + 1], scale=SM_SCALE),
                                waits=[("pe", mm), ("dve", bias_rdy)], inc="act")
                            ps_s_reader[sb] = ("act", ex)
                        st1 = E.emit("vector", lambda e, j=j, g=g, L=L: e.scalar_tensor_tensor(
                            out=a_p[j][:, 0:L], in0=a_iota[:, 0:L],
                            scalar=sb_theta[:, 4 * g + j:4 * g + j + 1],
                            in1=a_p[j][:, 0:L], op0=ALU.is_le, op1=ALU.mult,
                            accum_out=a_l[:, j:j + 1]),
                            waits=[("act", E.cnt["act"]), ("gdma", ld_iota)], inc="dve")
                        st2 = E.emit("vector", lambda e, j=j: e.reciprocal(
                            out=a_rl[:, j:j + 1], in_=a_l[:, j:j + 1]),
                            waits=[("dve", st1)], inc="dve")
                        st3 = E.emit("vector", lambda e, j=j, L=L: e.tensor_scalar(
                            out=a_p[j][:, 0:L], in0=a_p[j][:, 0:L],
                            scalar1=a_rl[:, j:j + 1], scalar2=None, op0=ALU.mult),
                            waits=[("dve", st2)], inc="dve")
                        p_ready.append(st3)
                    exp_prev = E.cnt["act"]
                    for tile in range(NT):
                        tb = tile % 2
                        tr = None
                        for j in range(4):
                            w = [("dve", p_ready[j])]
                            if j == 0:
                                w.append(pt_evac[tb])
                            def tf(e, tile=tile, j=j, tb=tb):
                                return e.transpose(ps_pt[tb][:, j * 128:(j + 1) * 128],
                                                   in_=a_p[j][:, tile * 128:(tile + 1) * 128],
                                                   identity=sb_identb[:])
                            tr = E.emit("tensor", tf, waits=w, inc="pe")
                        ev = E.emit("vector", lambda e, tb=tb: e.tensor_copy(
                            out=a_pt[tb][:], in_=ps_pt[tb][:]),
                            waits=[("pe", tr), pt_cons[tb]], inc="dve")
                        pt_evac[tb] = ("dve", ev)
                        def avf(e, kb=kb, tile=tile, tb=tb, NT=NT):
                            return e.matmul(ps_o[:], lhsT=a_v[kb][:, tile, :],
                                            rhs=a_pt[tb][:],
                                            start=(tile == 0), stop=(tile == NT - 1),
                                            skip_group_check=True)
                        w = [("dve", ev), ("sdma", ldv)]
                        if tile == 0:
                            w.append(o_evac)
                        av = E.emit("tensor", avf, waits=w, inc="pe")
                        pt_cons[tb] = ("pe", av)
                        last_av = av
                    ob = h % 2
                    oe = E.emit("scalar", lambda e, ob=ob: e.copy(
                        out=a_o[ob][:], in_=ps_o[:]),
                        waits=[("pe", last_av), o_store[ob]], inc="act")
                    o_evac = ("act", oe)
                    ost = E.dma("g", s_o[g, h], a_o[ob][:], waits=[("act", oe)])
                    o_store[ob] = ("gdma", ost)
                    if h == 4 * kh + 3:
                        k_used[kb] = ("pe", last_av)
                        v_used[kb] = ("pe", last_av)

    for g in range(4):
        with contextlib.ExitStack() as eso:
            b_hh = eso.enter_context(nc.sbuf_tensor(f"b_hh{g}", [128, ICN, TB], BF16))

            with contextlib.ExitStack() as es:
                blk = es.enter_context(nc.Block(f"b_ffn{g}a"))
                E.block = blk
                E.last_dmat = 0
                c_xnf = es.enter_context(nc.sbuf_tensor(f"b_xnf{g}", [128, HCN, TB], BF16))
                c_gst = _sbl(nc, es, f"b_gst{g}", 2, [128, HCN + 1, 128], BF16)
                c_ust = _sbl(nc, es, f"b_ust{g}", 2, [128, HCN + 1, 128], BF16)
                c_gA = es.enter_context(nc.sbuf_tensor(f"b_gA{g}", [128, HCN, R], BF16))
                c_uA = es.enter_context(nc.sbuf_tensor(f"b_uA{g}", [128, HCN, R], BF16))
                c_ABf = es.enter_context(nc.sbuf_tensor(f"b_ABf{g}", [128, HCN, R], F32))
                c_u1g = es.enter_context(nc.sbuf_tensor(f"b_u1g{g}", [R, TB], BF16))
                c_u1u = es.enter_context(nc.sbuf_tensor(f"b_u1u{g}", [R, TB], BF16))
                c_sx = es.enter_context(nc.sbuf_tensor(f"b_sx{g}", [1, TB], BF16))
                c_sxb = es.enter_context(nc.sbuf_tensor(f"b_sxb{g}", [128, TB], F32))
                c_sg = _sbl(nc, es, f"b_sg{g}", 2, [128, TB], BF16)
                c_gv = _sbl(nc, es, f"b_gv{g}", 2, [128, TB], F32)
                c_uv = _sbl(nc, es, f"b_uv{g}", 2, [128, TB], F32)
                ps_g = _psl(nc, es, f"ps_g{g}", 2, [128, TB], F32)
                ps_u = _psl(nc, es, f"ps_u{g}", 2, [128, TB], F32)
                ps_r = es.enter_context(nc.psum_tensor(f"ps_r{g}", [R, TB], F32))
                ps_x1 = es.enter_context(nc.psum_tensor(f"ps_x1{g}", [1, TB], F32))

                ld_xnf = E.dma("g", c_xnf[:], s_xnfq[g])
                lg = E.dma("g", c_ABf[:], gA)
                ag = E.emit("scalar", lambda e: e.copy(out=c_gA[:], in_=c_ABf[:]),
                            waits=[("gdma", lg)], inc="act")
                lu = E.dma("g", c_ABf[:], uA, waits=[("act", ag)])
                au = E.emit("scalar", lambda e: e.copy(out=c_uA[:], in_=c_ABf[:]),
                            waits=[("gdma", lu)], inc="act")

                mmx = E.mmg(ps_x1[:], [(sb_ones[:], c_xnf[:, hc, :]) for hc in range(HCN)],
                            waits=[("gdma", ld_xnf)])
                asx = E.emit("scalar", lambda e: e.copy(out=c_sx[:], in_=ps_x1[:]),
                             waits=[("pe", mmx)], inc="act")
                mmb = E.mmg(ps_g[0][:], [(sb_onesr[:], c_sx[:])], waits=[("act", asx)])
                asxb = E.emit("scalar", lambda e: e.copy(out=c_sxb[:], in_=ps_g[0][:]),
                              waits=[("pe", mmb)], inc="act")
                mm1 = E.mmg(ps_r[:], [(c_gA[:, hc, :], c_xnf[:, hc, :]) for hc in range(HCN)],
                            waits=[("act", ag), ("gdma", ld_xnf)])
                au1g = E.emit("scalar", lambda e: e.activation(
                    out=c_u1g[:], in_=ps_r[:], func=AF.Copy, scale=0.02),
                    waits=[("pe", mm1)], inc="act")
                mm2 = E.mmg(ps_r[:], [(c_uA[:, hc, :], c_xnf[:, hc, :]) for hc in range(HCN)],
                            waits=[("act", au), ("act", au1g)])
                au1u = E.emit("scalar", lambda e: e.activation(
                    out=c_u1u[:], in_=ps_r[:], func=AF.Copy, scale=0.02),
                    waits=[("pe", mm2)], inc="act")

                gv_used = [Z, Z]
                uv_used = [Z, Z]
                sg_used = [Z, Z]
                hh_done = [Z, Z]
                ps_g_rd = [("act", asxb), Z]
                ps_u_rd = [Z, Z]
                strip_cons_g = [Z, Z]
                strip_cons_u = [Z, Z]
                for ic in range(ICN):
                    bi = ic % 2
                    lds_g = E.dma("s", c_gst[bi][:], s_gqT[ic],
                                  waits=[("sdma", stage_done["gq"]), strip_cons_g[bi]])
                    lds_u = E.dma("s", c_ust[bi][:], s_uqT[ic],
                                  waits=[("sdma", stage_done["uq"]), strip_cons_u[bi]])
                    pairs_g = [(c_gst[bi][:, hc, :], c_xnf[:, hc, :]) for hc in range(HCN)]
                    pairs_g.append((c_gst[bi][0:32, HCN, :], c_u1g[:]))
                    mmgi = E.mmg(ps_g[bi][:], pairs_g,
                                 waits=[("sdma", lds_g), ("act", au1g), ps_g_rd[bi]])
                    strip_cons_g[bi] = ("pe", mmgi)
                    gv = E.emit("vector", lambda e, bi=bi, ic=ic: e.scalar_tensor_tensor(
                        out=c_gv[bi][:], in0=c_sxb[:],
                        scalar=sb_cgc[:, ic:ic + 1], in1=ps_g[bi][:],
                        op0=ALU.mult, op1=ALU.add),
                        waits=[("pe", mmgi), ("act", asxb), gv_used[bi]], inc="dve")
                    ps_g_rd[bi] = ("dve", gv)
                    sl = E.emit("scalar", lambda e, bi=bi: e.activation(
                        out=c_sg[bi][:], in_=c_gv[bi][:], func=AF.Silu),
                        waits=[("dve", gv), sg_used[bi]], inc="act")
                    gv_used[bi] = ("act", sl)
                    pairs_u = [(c_ust[bi][:, hc, :], c_xnf[:, hc, :]) for hc in range(HCN)]
                    pairs_u.append((c_ust[bi][0:32, HCN, :], c_u1u[:]))
                    mmui = E.mmg(ps_u[bi][:], pairs_u,
                                 waits=[("sdma", lds_u), ("act", au1u), ps_u_rd[bi]])
                    strip_cons_u[bi] = ("pe", mmui)
                    uv = E.emit("vector", lambda e, bi=bi, ic=ic: e.scalar_tensor_tensor(
                        out=c_uv[bi][:], in0=c_sxb[:],
                        scalar=sb_cuc[:, ic:ic + 1], in1=ps_u[bi][:],
                        op0=ALU.mult, op1=ALU.add),
                        waits=[("pe", mmui), ("act", asxb), uv_used[bi]], inc="dve")
                    ps_u_rd[bi] = ("dve", uv)
                    hd = E.emit("vector", lambda e, bi=bi, ic=ic: e.tensor_tensor(
                        out=b_hh[:, ic, :], in0=c_uv[bi][:], in1=c_sg[bi][:], op=ALU.mult),
                        waits=[("dve", uv), ("act", sl), hh_done[bi]], inc="dve")
                    uv_used[bi] = ("dve", hd)
                    sg_used[bi] = ("dve", hd)
                    hh_done[bi] = ("dve", hd)
                hh_all = E.cnt["dve"]

            with contextlib.ExitStack() as es:
                blk = es.enter_context(nc.Block(f"b_ffn{g}b"))
                E.block = blk
                E.last_dmat = 0
                d_ost = es.enter_context(nc.sbuf_tensor(f"d_ost{g}", [128, NH, TB], BF16))
                d_dst = _sbl(nc, es, f"d_dst{g}", 2, [128, ICN + 1, 128], BF16)
                d_wst = _sbl(nc, es, f"d_wst{g}", 2, [128, NH, 128], BF16)
                d_dA = es.enter_context(nc.sbuf_tensor(f"d_dA{g}", [128, ICN, R], BF16))
                d_dAf = es.enter_context(nc.sbuf_tensor(f"d_dAf{g}", [128, 43, R], F32))
                d_u1d = es.enter_context(nc.sbuf_tensor(f"d_u1d{g}", [R, TB], BF16))
                d_sh = es.enter_context(nc.sbuf_tensor(f"d_sh{g}", [1, TB], BF16))
                d_shb = es.enter_context(nc.sbuf_tensor(f"d_shb{g}", [128, TB], F32))
                d_f = _sbl(nc, es, f"d_f{g}", 2, [128, TB], F32)
                d_x = _sbl(nc, es, f"d_x{g}", 2, [128, 4, 128], F32)
                d_out = _sbl(nc, es, f"d_out{g}", 2, [128, 4, 128], F32)
                ps_d = _psl(nc, es, f"ps_d{g}", 2, [128, TB], F32)
                ps_t = es.enter_context(nc.psum_tensor(f"ps_t{g}", [128, TB], F32))
                ps_h1 = es.enter_context(nc.psum_tensor(f"ps_h1{g}", [1, TB], F32))
                ps_r2 = es.enter_context(nc.psum_tensor(f"ps_r2{g}", [R, TB], F32))

                la1 = E.dma("g", d_dAf[:], dA[:, 0:43, :])
                aa1 = E.emit("scalar", lambda e: e.copy(out=d_dA[:, 0:43, :], in_=d_dAf[:]),
                             waits=[("gdma", la1)], inc="act")
                la2 = E.dma("g", d_dAf[:], dA[:, 43:ICN, :], waits=[("act", aa1)])
                aa2 = E.emit("scalar", lambda e: e.copy(out=d_dA[:, 43:ICN, :], in_=d_dAf[:]),
                             waits=[("gdma", la2)], inc="act")
                o_ready = 0
                for h in range(NH):
                    o_ready = E.dma("g", d_ost[:, h, :], s_o[g, h])

                mmsh = E.mmg(ps_h1[:], [(sb_ones[:], b_hh[:, ic, :]) for ic in range(ICN)],
                             waits=[("dve", hh_all)])
                ash = E.emit("scalar", lambda e: e.copy(out=d_sh[:], in_=ps_h1[:]),
                             waits=[("pe", mmsh)], inc="act")
                mmshb = E.mmg(ps_d[0][:], [(sb_onesr[:], d_sh[:])], waits=[("act", ash)])
                ashb = E.emit("scalar", lambda e: e.copy(out=d_shb[:], in_=ps_d[0][:]),
                              waits=[("pe", mmshb)], inc="act")
                mmu1 = E.mmg(ps_r2[:], [(d_dA[:, ic, :], b_hh[:, ic, :]) for ic in range(ICN)],
                             waits=[("act", aa2), ("dve", hh_all)])
                au1d = E.emit("scalar", lambda e: e.activation(
                    out=d_u1d[:], in_=ps_r2[:], func=AF.Copy, scale=0.02),
                    waits=[("pe", mmu1)], inc="act")

                ps_d_rd = [("act", ashb), Z]
                dstrip_cons = [Z, Z]
                wstrip_cons = [Z, Z]
                f_used = [Z, Z]
                xb_used = [Z, Z]
                out_used = [Z, Z]
                tr_done = 0
                add_done = 0
                for hc in range(HCN):
                    bi = hc % 2
                    lds_d = E.dma("s", d_dst[bi][:], s_dqT[hc],
                                  waits=[("sdma", stage_done["dq"]), dstrip_cons[bi]])
                    lds_w = E.dma("s", d_wst[bi][:], s_woT[hc],
                                  waits=[("sdma", stage_done["wo"]), wstrip_cons[bi]])
                    ldx = E.dma("g", d_x[bi][:],
                                xq[g * TB:(g + 1) * TB, hc * 128:(hc + 1) * 128]
                                .rearrange("(j p) f -> p j f", p=128),
                                waits=[xb_used[bi]])
                    pairs = [(d_dst[bi][:, ic, :], b_hh[:, ic, :]) for ic in range(ICN)]
                    pairs.append((d_dst[bi][0:32, ICN, :], d_u1d[:]))
                    pairs += [(d_wst[bi][:, oc, :], d_ost[:, oc, :]) for oc in range(NH)]
                    mmd = E.mmg(ps_d[bi][:], pairs,
                                waits=[("sdma", lds_d), ("sdma", lds_w),
                                       ("gdma", o_ready), ("act", au1d),
                                       ps_d_rd[bi]])
                    dstrip_cons[bi] = ("pe", mmd)
                    wstrip_cons[bi] = ("pe", mmd)
                    ev = E.emit("vector", lambda e, bi=bi, hc=hc: e.scalar_tensor_tensor(
                        out=d_f[bi][:], in0=d_shb[:],
                        scalar=sb_cdc[:, hc:hc + 1], in1=ps_d[bi][:],
                        op0=ALU.mult, op1=ALU.add),
                        waits=[("pe", mmd), ("act", ashb), f_used[bi]], inc="dve")
                    ps_d_rd[bi] = ("dve", ev)
                    tr = None
                    for j in range(4):
                        def tf(e, bi=bi, j=j):
                            return e.transpose(ps_t[:, j * 128:(j + 1) * 128],
                                               in_=d_f[bi][:, j * 128:(j + 1) * 128],
                                               identity=sb_identf[:])
                        tr = E.emit("tensor", tf,
                                    waits=[("dve", ev)] + ([("dve", add_done)] if j == 0 else []),
                                    inc="pe")
                    tr_done = tr
                    f_used[bi] = ("pe", tr)
                    ad = E.emit("vector", lambda e, bi=bi: e.tensor_tensor(
                        out=d_out[bi][:], in0=ps_t[:].rearrange("p (j f) -> p j f", j=4),
                        in1=d_x[bi][:], op=ALU.add),
                        waits=[("pe", tr), ("gdma", ldx), out_used[bi]], inc="dve")
                    add_done = ad
                    xb_used[bi] = ("dve", ad)
                    ys = E.dma("g", y[g * TB:(g + 1) * TB, hc * 128:(hc + 1) * 128]
                               .rearrange("(j p) f -> p j f", p=128), d_out[bi][:],
                               waits=[("dve", ad)])
                    out_used[bi] = ("gdma", ys)

    with contextlib.ExitStack() as es:
        blk = es.enter_context(nc.Block("b_final"))
        E.block = blk
        gd = E.cnt["gdma"]
        sd = E.cnt["sdma"]
        sem = E.sem
        blk.gpsimd(lambda e, gd=gd: e.wait_ge(sem["gdma"], gd))
        blk.sync(lambda e, sd=sd: e.wait_ge(sem["sdma"], sd))

    prog_es.close()
    return nc


_NC_CACHE = None


def _get_nc():
    global _NC_CACHE
    if _NC_CACHE is None:
        _NC_CACHE = build_program()
    return _NC_CACHE


def _llama3_cos_sin():
    d = np.arange(0, HD, 2, dtype=np.float64) / HD
    inv = 1.0 / (500000.0 ** d)
    wavelen = 2.0 * np.pi / inv
    low_wl = 8192 / 1.0
    high_wl = 8192 / 4.0
    inv_scaled = np.where(wavelen > low_wl, inv / 8.0, inv)
    smooth = (8192 / wavelen - 1.0) / (4.0 - 1.0)
    smoothed = (1.0 - smooth) / 8.0 * inv + smooth * inv
    is_medium = (wavelen <= low_wl) & (wavelen >= high_wl)
    invf = np.where(is_medium, smoothed, inv_scaled).astype(np.float32)
    pos = np.arange(T, dtype=np.float32)
    fr = pos[None, :] * invf[:, None].astype(np.float32)
    return np.cos(fr).astype(np.float32), np.sin(fr).astype(np.float32)


def kernel(L0_attn_x, L0_attn_Wq, L0_attn_Wk, L0_attn_Wv, L0_attn_Wo,
           L0_ffn_gate_q, L0_ffn_gate_scale, L0_ffn_gate_A, L0_ffn_gate_B,
           L0_ffn_up_q, L0_ffn_up_scale, L0_ffn_up_A, L0_ffn_up_B,
           L0_ffn_down_q, L0_ffn_down_scale, L0_ffn_down_A, L0_ffn_down_B):
    x = np.ascontiguousarray(np.asarray(L0_attn_x, dtype=np.float32))
    wq = np.ascontiguousarray(np.asarray(L0_attn_Wq, dtype=np.float32))
    wk = np.ascontiguousarray(np.asarray(L0_attn_Wk, dtype=np.float32))
    wv = np.ascontiguousarray(np.asarray(L0_attn_Wv, dtype=np.float32))
    wo = np.ascontiguousarray(np.asarray(L0_attn_Wo, dtype=np.float32))
    gq = np.ascontiguousarray(np.asarray(L0_ffn_gate_q, dtype=np.int32))
    uq = np.ascontiguousarray(np.asarray(L0_ffn_up_q, dtype=np.int32))
    dq = np.ascontiguousarray(np.asarray(L0_ffn_down_q, dtype=np.int32))
    gs = np.asarray(L0_ffn_gate_scale, dtype=np.float32)
    us = np.asarray(L0_ffn_up_scale, dtype=np.float32)
    ds = np.asarray(L0_ffn_down_scale, dtype=np.float32)
    gA_ = np.ascontiguousarray(np.asarray(L0_ffn_gate_A, np.float32).reshape(HCN, 128, R).transpose(1, 0, 2))
    uA_ = np.ascontiguousarray(np.asarray(L0_ffn_up_A, np.float32).reshape(HCN, 128, R).transpose(1, 0, 2))
    dA_ = np.ascontiguousarray(np.asarray(L0_ffn_down_A, np.float32).reshape(ICN, 128, R).transpose(1, 0, 2))
    gB_ = np.ascontiguousarray(np.asarray(L0_ffn_gate_B, np.float32))
    uB_ = np.ascontiguousarray(np.asarray(L0_ffn_up_B, np.float32))
    dB_ = np.ascontiguousarray(np.asarray(L0_ffn_down_B, np.float32))

    cosk_, sink_ = _llama3_cos_sin()
    iota_ = np.ascontiguousarray(np.broadcast_to(np.arange(T, dtype=np.float32), (128, T)))
    identf_ = np.eye(128, dtype=np.float32)
    identb_ = np.eye(128).astype(BF)
    ones_ = np.ones((128, 1), dtype=BF)
    onesr_ = np.ones((1, 128), dtype=BF)
    gs_pc_ = np.ascontiguousarray(gs.reshape(ICN, 128).T)
    us_pc_ = np.ascontiguousarray(us.reshape(ICN, 128).T)
    ds_pc_ = np.ascontiguousarray(ds.reshape(HCN, 128).T)

    nc = _get_nc()
    in_maps = []
    for core in range(8):
        b, z = core // 2, core % 2
        own = OWN[z]
        tok = np.concatenate([np.arange(o * TB, (o + 1) * TB) for o in own])
        th = np.empty((128, 16), dtype=np.float32)
        p = np.arange(128, dtype=np.float32)
        for g in range(4):
            for j in range(4):
                th[:, 4 * g + j] = own[g] * TB + j * 128 + p
        in_maps.append(dict(
            x_full=x[b], xq=np.ascontiguousarray(x[b][tok]),
            wq_w=wq, wk_w=wk, wv_w=wv, wo_w=wo,
            gq_w=gq, uq_w=uq, dq_w=dq,
            gs_pc=gs_pc_, us_pc=us_pc_, ds_pc=ds_pc_,
            gA=gA_, uA=uA_, dA=dA_, gB=gB_, uB=uB_, dB=dB_,
            cosk=cosk_, sink=sink_,
            cosq=np.ascontiguousarray(cosk_[:, tok]),
            sinq=np.ascontiguousarray(sink_[:, tok]),
            iota=iota_, theta=th, identf=identf_, identb=identb_,
            onescol=ones_, onesrow=onesr_,
        ))

    res = run_bass_kernel_spmd(nc, in_maps, core_ids=list(range(8)))
    out = np.empty((4, T, H), dtype=np.float32)
    for core in range(8):
        b, z = core // 2, core % 2
        own = OWN[z]
        yc = np.asarray(res.results[core]["y"])
        for g in range(4):
            out[b, own[g] * TB:(own[g] + 1) * TB, :] = yc[g * TB:(g + 1) * TB, :]
    return out


if __name__ == "__main__":
    import time
    t0 = time.time()
    nc = build_program()
    print("built ok in %.1fs" % (time.time() - t0))
